# version 08 (final)
# baseline (speedup 1.0000x reference)
import numpy as np
import ml_dtypes
import concourse.bass as bass
import concourse.mybir as mybir
from concourse.bass import AP
from concourse.bass_utils import run_bass_kernel_spmd
from contextlib import ExitStack

F32 = mybir.dt.float32
BF16 = mybir.dt.bfloat16
AF = mybir.ActivationFunctionType
ALU = mybir.AluOpType
BF = ml_dtypes.bfloat16

DM = 2048
DI = 4096
DS = 16
RK = 128
L = 2048
LC = 512
NCH = L // LC
NDT = DM // 128
NDB = DI // 128
NJ = 8192 // 128
EPS = 1e-5

ENG_SEM = {"sync": "sDMA", "tensor": "sPE", "vector": "sDVE",
           "scalar": "sACT", "gpsimd": "sGP"}


def _hadamard(n):
    H = np.array([[1.0]], np.float32)
    while H.shape[0] < n:
        H = np.block([[H, H], [H, -H]])
    return np.ascontiguousarray(H, np.float32)


DMA_RING = {"sync": ("qS", 10), "scalar": ("qA", 8), "gpsimd": ("qG", 8)}


class Prog:
    ENGS = ("sync", "tensor", "vector", "scalar", "gpsimd")

    def __init__(self):
        self.ev = {e: [] for e in self.ENGS}
        self.cnt = {s: 0 for s in ("sDMA", "sACTD", "sPE", "sDVE", "sACT",
                                   "sGP", "sGPD")}
        for base, K in DMA_RING.values():
            for i in range(K):
                self.cnt[f"{base}{i}"] = 0
        self.dmacnt = {e: 0 for e in DMA_RING}
        self.last_dma = {}
        self.barriers = {}
        self.tags = {}

    def queue_barrier(self, eng):
        return list(self.last_dma.get(eng, {}).values())

    def emit(self, eng, fn, waits=(), tag=None, dma=False):
        wl = []
        waits = list(waits)
        i = 0
        while i < len(waits):
            w = waits[i]
            if isinstance(w, str) and w.startswith("@"):
                name = w[1:]
                if name in self.barriers:
                    waits[i:i + 1] = self.barriers[name]
                else:
                    waits[i:i + 1] = self.queue_barrier(name)
                continue
            i += 1
        for w in waits:
            if w is None:
                continue
            if w not in self.tags:
                raise KeyError(f"unknown tag {w} (emitting on {eng})")
            s, v, weng = self.tags[w]
            if weng == eng and not s.startswith("q") and not dma:
                continue
            wl.append((s, v))
        incspec = None
        if dma:
            base, K = DMA_RING[eng]
            idx = self.dmacnt[eng]
            self.dmacnt[eng] = idx + 1
            sem = f"{base}{idx % K}"
            v = 16 * (idx // K + 1)
            if idx >= K:
                wl.append((sem, v - 16))
            self.cnt[sem] = v
            incspec = (sem, 16)
            if tag is None:
                tag = f"_dma_{eng}_{idx}"
            self.tags[tag] = (sem, v, eng)
            self.last_dma.setdefault(eng, {})[idx % K] = tag
        elif tag is not None:
            sem = ENG_SEM[eng]
            self.cnt[sem] += 1
            incspec = (sem, 1)
            self.tags[tag] = (sem, self.cnt[sem], eng)
        self.ev[eng].append((fn, wl, incspec))

    def alias(self, new, old):
        self.tags[new] = self.tags[old]

    def has(self, tag):
        return tag in self.tags


def build_nc():
    nc = bass.Bass("TRN2", target_bir_lowering=False, debug=False)

    def din(name, shape, dt=F32):
        return nc.dram_tensor(name, shape, dt, kind="ExternalInput").ap()

    d_x = din("xT", [DM, L])
    d_win = din("W_in", [DM, 8192])
    d_wxp = din("W_xp", [DI, 160])
    d_wdt = din("W_dt", [RK, DI])
    d_wout = din("W_out", [DI, DM])
    d_wupT = din("W_upT", [DM, 8192])
    d_wgateT = din("W_gateT", [DM, 8192])
    d_wdownT = din("W_downT", [8192, DM])
    d_alog = din("A_log_r", [128, NDB, DS])
    d_convw = din("conv_w_r", [128, NDB, 4])
    d_convb = din("conv_b_r", [128, NDB])
    d_bdt = din("b_dt_r", [128, NDB])
    d_D = din("D_r", [128, NDB])
    d_wnorm = din("w_norm_r", [128, NDT])
    d_suug = din("SU_ug_r", [128, NDT])
    d_sudp = din("SU_dp_r", [128, NJ])
    d_wsup = din("Wsc_up_r", [128, NJ])
    d_wsgt = din("Wsc_gt_r", [128, NJ])
    d_wsdp = din("Wsc_dp_r", [128, NDT])
    d_h128 = din("H128c", [128, 128], BF16)
    d_h64 = din("H64", [64, 64], BF16)
    d_iden = din("iden128", [128, 128], BF16)
    d_onesb = din("ones_bf", [128, 1], BF16)
    d_onesf = din("ones1x128_f", [1, 128], F32)
    d_misc = din("misc_c", [128, 4])

    d_y = nc.dram_tensor("y_out", [DM, L], F32, kind="ExternalOutput").ap()

    def dint(name, shape, dt=BF16):
        return nc.dram_tensor(name, shape, dt, kind="Internal")

    s_win = dint("s_win", [16, DM, 512]).ap()
    s_wout = dint("s_wout", [4, DI, 512]).ap()
    s_wup = dint("s_wup", [16, DM, 512]).ap()
    s_wgate = dint("s_wgate", [16, DM, 512]).ap()
    s_wxp = dint("s_wxp", [128, NDB, 160]).ap()
    s_wdt = dint("s_wdt", [RK, DI]).ap()
    s_T = dint("s_T", [64, 128, DM]).ap()
    s_G = dint("s_G", [4, 8192, 512]).ap()
    s_u = dint("s_u", [NDB, 128, 512]).ap()
    s_sr = dint("s_sr", [NDB, 128, 512]).ap()
    s_yb = dint("s_yb", [NDB, 128, 512]).ap()
    s_bc = dint("s_bc", [32, 512]).ap()

    P = Prog()

    with ExitStack() as ctx:
        def sb(name, shape, dt=F32):
            return ctx.enter_context(nc.sbuf_tensor(name, shape, dt))

        sx = [sb(f"sx{i}", [128, NDT, LC], BF16) for i in range(2)]
        xnb = sb("xnb", [128, NDT, LC], BF16)
        xf = sb("xf", [128, NDT, LC], BF16)
        u_ring = sb("u_ring", [128, 4, LC], BF16)
        sr_ring = sb("sr_ring", [128, 3, LC], BF16)
        yb_ring = sb("yb_ring", [128, 3, LC], BF16)
        brep = sb("brep", [128, DS, LC], BF16)
        crep = sb("crep", [128, DS, LC], BF16)
        zq = sb("zq", [128, 16, LC], BF16)
        wslab = [sb(f"wslab{i}", [128, 16, 512], BF16) for i in range(2)]
        wxp_sb = sb("wxp_sb", [128, NDB, 160], BF16)
        wdt_ring = sb("wdt_ring", [128, 3, 128], BF16)
        lnrows = sb("lnrows", [128, 3, LC], BF16)
        xs_ring = sb("xs_ring", [128, 2, 3 + LC], BF16)
        tailb = sb("tailb", [128, NDB, 3], BF16)
        delta_r = sb("delta_r", [128, 2, LC], BF16)
        du_r = sb("du_r", [128, 2, LC], BF16)
        dtr = sb("dtr", [128, LC], BF16)
        bcsb = sb("bcsb", [32, LC], BF16)
        dA_r = sb("dA_r", [128, 2, LC], BF16)
        dBu_r = sb("dBu_r", [128, 2, LC], BF16)
        htr = sb("htr", [128, 2, LC], BF16)
        p_r = sb("p_r", [128, 3, LC], BF16)
        hcarry = sb("hcarry", [128, 512])
        yd_r = sb("yd_r", [128, 2, LC])
        scrA = sb("scrA", [128, 4, LC])
        zu_r = sb("zu_r", [128, 2, LC], BF16)
        zg_r = sb("zg_r", [128, 2, LC], BF16)
        sq_r = zu_r
        xft = zg_r.ap()[:, 0, :]
        zq_flat = zq.ap().rearrange("p a b -> p (a b)")
        xf_flat = xf.ap().rearrange("p a b -> p (a b)")
        fold_in = [zq_flat[:, r * DM:(r + 1) * DM] for r in range(2)]
        fold_t = [zq_flat[:, 2 * DM + r * DM:2 * DM + (r + 1) * DM]
                  for r in range(2)]
        fold_s2 = [xf_flat[0:64, r * DM:(r + 1) * DM] for r in range(2)]
        fold_t2 = [xf_flat[0:64, 2 * DM + r * DM:2 * DM + (r + 1) * DM]
                   for r in range(2)]
        c_h128 = sb("c_h128", [128, 128], BF16)
        c_h64 = sb("c_h64", [64, 64], BF16)
        c_iden = sb("c_iden", [128, 128], BF16)
        c_onesb = sb("c_onesb", [128, 1], BF16)
        c_onesf = sb("c_onesf", [1, 128], F32)
        c_A = sb("c_A", [128, NDB, DS])
        c_convw = sb("c_convw", [128, NDB, 4])
        c_convb = sb("c_convb", [128, NDB])
        c_bdt = sb("c_bdt", [128, NDB])
        c_D = sb("c_D", [128, NDB])
        c_wnorm = sb("c_wnorm", [128, NDT])
        c_suug = sb("c_suug", [128, NDT])
        c_sudp = None
        c_susc = sb("c_susc", [128, NJ])
        c_ggsc = sb("c_ggsc", [128, NJ])
        c_dpsc = sb("c_dpsc", [128, NDT])
        c_misc = sb("c_misc", [128, 4])

        pbh = [ctx.enter_context(nc.psum_tensor(f"pb{i}", [128, 512], F32))
               for i in range(8)]
        pb = [t.ap() for t in pbh]

        sems = {s: ctx.enter_context(nc.semaphore(s)) for s in P.cnt}
        block = ctx.enter_context(nc.Block())

        W = lambda t: (t if P.has(t) else None)
        bank_guard = {i: None for i in range(8)}

        slab_state = {"n": 0}

        def slab_use(src_ap, extra_waits=()):
            n = slab_state["n"]
            slab_state["n"] = n + 1
            slot = n % 2
            P.emit("sync",
                   lambda e, s=slot, src=src_ap: e.dma_start(
                       out=wslab[s].ap(), in_=src),
                   waits=[W(f"slm{n-2}")] + list(extra_waits),
                   tag=f"sl{n}", dma=True)
            return slot, f"sl{n}", n

        const_loads = [
            (c_h128[:], d_h128), (c_h64[:], d_h64), (c_iden[:], d_iden),
            (c_onesb[:], d_onesb), (c_onesf[:], d_onesf),
            (scrA[:, 0, :], d_alog.rearrange("p a b -> p (a b)")),
            (c_convw[:], d_convw), (c_convb[:], d_convb),
            (c_bdt[:], d_bdt), (c_D[:], d_D), (c_wnorm[:], d_wnorm),
            (c_suug[:], d_suug), (scrA[:, 1, 0:NJ], d_sudp),
            (c_susc[:], d_wsup),
            (c_ggsc[:], d_wsgt), (c_dpsc[:], d_wsdp), (c_misc[:], d_misc),
        ]
        for i, (dst, src) in enumerate(const_loads):
            P.emit("sync", lambda e, dst=dst, src=src: e.dma_start(out=dst,
                                                                   in_=src),
                   tag=f"const{i}", dma=True)
        P.barriers = {"consts": P.queue_barrier("sync")}

        P.emit("vector", lambda e: e.memset(
            tailb.ap().rearrange("p a b -> p (a b)"), 0.0))
        P.emit("vector", lambda e: e.memset(hcarry[:], 0.0))
        P.emit("vector",
               lambda e: e.tensor_scalar(c_susc[:], c_susc[:],
                                         2.0 ** -1.5, None, ALU.mult),
               waits=["@consts"])
        P.emit("vector", lambda e: e.tensor_mul(c_susc[:], c_susc[:],
                                                scrA[:, 1, 0:NJ]))
        P.emit("vector",
               lambda e: e.tensor_scalar(c_ggsc[:], c_ggsc[:],
                                         2.0 ** -1.5, None, ALU.mult))
        P.emit("vector",
               lambda e: e.tensor_scalar(c_dpsc[:], c_dpsc[:],
                                         2.0 ** -2.5, None, ALU.mult),
               tag="scales")
        P.emit("scalar",
               lambda e: e.activation(c_A.ap().rearrange("p a b -> p (a b)"),
                                      scrA[:, 0, :], AF.Exp),
               waits=["@consts"], tag="Aexp")
        P.emit("vector",
               lambda e: e.tensor_scalar(c_A.ap().rearrange("p a b -> p (a b)"),
                                         c_A.ap().rearrange("p a b -> p (a b)"),
                                         -1.0, None, ALU.mult),
               waits=["Aexp"], tag="Aneg")

        def cast(dst, src, tag):
            P.emit("gpsimd", lambda e, d=dst, s=src: e.dma_start(out=d, in_=s),
                   tag=tag, dma=True)

        xv0 = d_x.rearrange("(t p) l -> p t l", p=128)[:, :, 0:LC]
        P.emit("gpsimd",
               lambda e: e.dma_start(
                   out=sx[0].ap().rearrange("p a b -> p (a b)"), in_=xv0),
               tag="xld0", dma=True)
        for pg in range(16):
            cast(s_win[pg], d_win[:, pg * 512:(pg + 1) * 512], f"cWin{pg}")
        cast(s_wxp, d_wxp.rearrange("(t p) c -> p t c", p=128), "cWxp")
        cast(s_wdt, d_wdt, "cWdt")
        for pg in range(4):
            cast(s_wout[pg], d_wout[:, pg * 512:(pg + 1) * 512], f"cWout{pg}")
        for pg in range(16):
            cast(s_wup[pg], d_wupT[:, pg * 512:(pg + 1) * 512], f"cWup{pg}")
        for pg in range(16):
            cast(s_wgate[pg], d_wgateT[:, pg * 512:(pg + 1) * 512],
                 f"cWgate{pg}")
        P.emit("sync", lambda e: e.dma_start(out=wxp_sb.ap(), in_=s_wxp),
               waits=["cWxp"], tag="ldWxp", dma=True)

        env = dict(
            P=P, W=W, nc=nc, pb=pb, bank_guard=bank_guard, slab_use=slab_use,
            sx=sx, xnb=xnb, xf=xf, u_ring=u_ring, sr_ring=sr_ring,
            yb_ring=yb_ring, brep=brep, crep=crep, zq=zq, wslab=wslab,
            wxp_sb=wxp_sb, wdt_ring=wdt_ring, lnrows=lnrows, xs_ring=xs_ring,
            tailb=tailb, delta_r=delta_r, du_r=du_r, dtr=dtr, bcsb=bcsb,
            dA_r=dA_r, dBu_r=dBu_r, htr=htr, p_r=p_r, hcarry=hcarry,
            yd_r=yd_r, scrA=scrA, zu_r=zu_r, zg_r=zg_r, sq_r=sq_r, xft=xft,
            fold_in=fold_in, fold_t=fold_t, fold_s2=fold_s2, fold_t2=fold_t2,
            c_h128=c_h128, c_h64=c_h64, c_iden=c_iden, c_onesb=c_onesb,
            c_onesf=c_onesf, c_A=c_A, c_convw=c_convw, c_convb=c_convb,
            c_bdt=c_bdt, c_D=c_D, c_wnorm=c_wnorm, c_suug=c_suug,
            c_susc=c_susc, c_ggsc=c_ggsc, c_dpsc=c_dpsc, c_misc=c_misc,
            d_x=d_x, d_y=d_y, d_wdownT=d_wdownT,
            s_win=s_win, s_wout=s_wout, s_wup=s_wup, s_wgate=s_wgate,
            s_wdt=s_wdt, s_T=s_T, s_G=s_G, s_u=s_u, s_sr=s_sr, s_yb=s_yb,
            s_bc=s_bc,
        )

        for c in range(NCH):
            _emit_front(env, c)
            _emit_window(env, c)
        _emit_drain(env)

        P.emit("sync", lambda e: None,
               waits=[f"stored{c}" for c in range(NCH)])

        def runner(engname):
            def body(eng):
                waited = {}
                for fn, wl, incspec in P.ev[engname]:
                    for s, v in wl:
                        if waited.get(s, -1) >= v:
                            continue
                        eng.wait_ge(sems[s], v)
                        waited[s] = v
                    inst = fn(eng)
                    if incspec is not None:
                        s, amt = incspec
                        inst.then_inc(sems[s], amt)
                    del inst
            return body

        block.sync(runner("sync"))
        block.tensor(runner("tensor"))
        block.vector(runner("vector"))
        block.scalar(runner("scalar"))
        block.gpsimd(runner("gpsimd"))

    return nc


def _emit_ln(g, pref, par, with_rms, pe_src_waits, out_fn, row0_waits=()):
    P, W, pb = g["P"], g["W"], g["pb"]
    sx, sq_r, scrA, lnrows = g["sx"][par], g["sq_r"], g["scrA"], g["lnrows"]
    c_onesb, c_onesf, c_misc = g["c_onesb"], g["c_onesf"], g["c_misc"]
    bank_guard = g["bank_guard"]
    row = scrA.ap()[0:1]

    for dt in range(NDT):
        P.emit("tensor",
               lambda e, dt=dt: e.matmul(pb[4][0:1, :], c_onesb[:],
                                         sx[:, dt, :],
                                         start=(dt == 0), stop=(dt == NDT - 1)),
               waits=(pe_src_waits + [W(bank_guard[4])]) if dt == 0 else [],
               tag=f"{pref}smm" if dt == NDT - 1 else None)
    for dt in range(NDT):
        P.emit("gpsimd",
               lambda e, dt=dt: e.tensor_mul(sq_r[:, dt % 2, :],
                                             sx[:, dt, :], sx[:, dt, :]),
               waits=(([W(f"{pref}gpw")] + pe_src_waits) if dt == 0 else []) +
                     [f"{pref}qmm{dt-2}" if dt >= 2 else None] +
                     ([W(f"{pref}pq{dt}")] if dt < 2 else []),
               tag=f"{pref}sq{dt}")
        P.emit("tensor",
               lambda e, dt=dt: e.matmul(pb[5][0:1, :], c_onesb[:],
                                         sq_r[:, dt % 2, :],
                                         start=(dt == 0), stop=(dt == NDT - 1)),
               waits=[f"{pref}sq{dt}",
                      W(bank_guard[5]) if dt == 0 else None],
               tag=f"{pref}qmm{dt}")
    P.emit("vector", lambda e: e.tensor_scalar(row[:, 0, :], pb[4][0:1, :],
                                               1.0 / DM, None, ALU.mult),
           waits=[f"{pref}smm"] + list(row0_waits), tag=f"{pref}sumrd")
    P.emit("vector", lambda e: e.tensor_scalar(row[:, 1, :], pb[5][0:1, :],
                                               1.0 / DM, None, ALU.mult),
           waits=[f"{pref}qmm{NDT-1}"], tag=f"{pref}qrd")
    P.emit("vector", lambda e: e.tensor_mul(row[:, 2, :], row[:, 0, :],
                                            row[:, 0, :]))
    P.emit("vector", lambda e: e.tensor_sub(row[:, 3, :], row[:, 1, :],
                                            row[:, 2, :]), tag=f"{pref}var")
    if with_rms:
        P.emit("vector", lambda e: e.tensor_scalar(row[:, 1, :], row[:, 3, :],
                                                   float(EPS), None, ALU.add))
        P.emit("vector", lambda e: e.reciprocal(row[:, 2, :], row[:, 1, :]))
        P.emit("vector", lambda e: e.tensor_mul(row[:, 1, :], row[:, 3, :],
                                                row[:, 2, :]),
               tag=f"{pref}E")
    P.emit("scalar", lambda e: e.activation(row[:, 2, :], row[:, 3, :], AF.Ln,
                                            bias=c_misc[0:1, 0:1]),
           waits=[f"{pref}E" if with_rms else f"{pref}var", "@consts"])
    P.emit("scalar", lambda e: e.activation(row[:, 3, :], row[:, 2, :], AF.Exp,
                                            scale=c_misc[0:1, 2:3]),
           tag=f"{pref}rstd")
    if with_rms:
        P.emit("scalar", lambda e: e.activation(row[:, 2, :], row[:, 1, :],
                                                AF.Ln, bias=c_misc[0:1, 0:1]),
               waits=[f"{pref}E"])
        P.emit("scalar", lambda e: e.activation(row[:, 1, :], row[:, 2, :],
                                                AF.Exp,
                                                scale=c_misc[0:1, 2:3]),
               tag=f"{pref}rmsr")
    P.alias(f"{pref}r0", f"{pref}rstd")
    P.emit("vector", lambda e: e.tensor_mul(row[:, 2, :], row[:, 0, :],
                                            row[:, 3, :]),
           waits=[f"{pref}rstd",
                  f"{pref}rmsr" if with_rms else None],
           tag=f"{pref}r1")
    rows_bc = [(3, f"{pref}r0"), (2, f"{pref}r1")]
    if with_rms:
        rows_bc.append((1, f"{pref}rmsr"))
    nbr = len(rows_bc)
    for ri, (src_i, wtag) in enumerate(rows_bc):
        P.emit("tensor",
               lambda e, src_i=src_i: e.matmul(pb[4], c_onesf[:],
                                               row[:, src_i, :],
                                               start=True, stop=True),
               waits=[wtag, f"{pref}sumrd" if ri == 0 else f"{pref}bcp{ri-1}"],
               tag=f"{pref}bmm{ri}")
        P.emit("vector",
               lambda e, ri=ri: e.tensor_copy(lnrows[:, ri, :], pb[4]),
               waits=[f"{pref}bmm{ri}"], tag=f"{pref}bcp{ri}")
    P.alias(f"{pref}bcplast", f"{pref}bcp{nbr-1}")
    bank_guard[4] = f"{pref}bcplast"
    bank_guard[5] = f"{pref}qrd"
    for dt in range(NDT):
        P.emit("vector",
               lambda e, dt=dt: e.tensor_mul(scrA[:, 0, :], sx[:, dt, :],
                                             lnrows[:, 0, :]),
               waits=([f"{pref}bcplast"] + list(row0_waits)) if dt == 0 else [],
               tag=f"{pref}srd{dt}")
        P.emit("vector",
               lambda e, dt=dt: e.tensor_sub(scrA[:, 1, :], scrA[:, 0, :],
                                             lnrows[:, 1, :]))
        out_fn(dt)


def _emit_wout_tail(g, c):
    P, W, pb = g["P"], g["W"], g["pb"]
    bank_guard, slab_use = g["bank_guard"], g["slab_use"]
    sx, xf = g["sx"], g["xf"]
    yb_ring, wslab = g["yb_ring"], g["wslab"]
    scrA, xft = g["scrA"], g["xft"]
    c_h128, c_suug = g["c_h128"], g["c_suug"]
    s_wout, s_yb = g["s_wout"], g["s_yb"]
    fc = c - 1
    fpar = fc % 2

    for r in range(2):
        for kh in range(2):
            slabinfo = {}
            for half in range(2):
                pg = 2 * r + half
                extra = [f"cWout{pg}" if c == 1 else None]
                slabinfo[half] = slab_use(
                    s_wout[pg].rearrange("(t p) c -> p t c", p=128)
                    [:, kh * 16:(kh + 1) * 16, :], extra)
            for k16 in range(16):
                ku = r * 32 + kh * 16 + k16
                k = kh * 16 + k16
                P.emit("scalar",
                       lambda e, ku=ku, k=k: e.dma_start(
                           out=yb_ring[:, ku % 3, :], in_=s_yb[k]),
                       waits=[f"ybst{fc}_{k}", W(f"wo{fc}_{ku-3}")],
                       tag=f"ybld{fc}_{ku}", dma=True)
                for half in range(2):
                    sl, lt, n = slabinfo[half]
                    for po in range(4):
                        bank = half * 4 + po
                        st = (kh == 0 and k16 == 0)
                        sp = (kh == 1 and k16 == 15)
                        wts = []
                        if k16 == 0:
                            wts.append(lt)
                        if half == 0 and po == 0:
                            wts.append(f"ybld{fc}_{ku}")
                        if st:
                            wts.append(W(bank_guard[bank]))
                        tag = None
                        if po == 3:
                            if half == 1:
                                tag = f"wo{fc}_{ku}"
                            elif k16 == 15:
                                tag = f"woA{fc}_{ku}"
                        P.emit("tensor",
                               lambda e, sl=sl, k16=k16, po=po, bank=bank,
                               ku=ku, st=st, sp=sp: e.matmul(
                                   pb[bank],
                                   wslab[sl][:, k16, po * 128:(po + 1) * 128],
                                   yb_ring[:, ku % 3, :],
                                   start=st, stop=sp),
                               waits=wts, tag=tag)
                if k16 == 15:
                    P.alias(f"slm{slabinfo[0][2]}", f"woA{fc}_{ku}")
                    P.alias(f"slm{slabinfo[1][2]}", f"wo{fc}_{ku}")
        for bank in range(8):
            dt = 8 * r + bank
            P.emit("vector",
                   lambda e, dt=dt, bank=bank: e.tensor_add(
                       sx[fpar][:, dt, :], sx[fpar][:, dt, :], pb[bank]),
                   waits=[f"wo{fc}_{r*32+31}"],
                   tag=f"x2ev{fc}_{dt}")
            bank_guard[bank] = f"x2ev{fc}_{dt}"

    if P.has(f"z{c-2}_63"):
        P.alias(f"ln2_{fc}gpw", f"z{c-2}_63")
    else:
        P.alias(f"ln2_{fc}gpw", "Aneg")
    P.alias(f"ln2_{fc}pq0", f"ln1_{fc}qmm14")
    P.alias(f"ln2_{fc}pq1", f"ln1_{fc}qmm15")

    def ln2_out(dt):
        w0 = []
        if dt == 0:
            w0 = [W(f"xffree{c-2}"), "@sync" if c == 1 else None,
                  W("f2mm127")]
        P.emit("vector",
               lambda e, dt=dt: e.tensor_scalar(
                   xf[:, dt, :], scrA[:, 1, :], c_suug[:, dt:dt + 1],
                   None, ALU.mult),
               waits=w0, tag=f"ln2_{fc}out{dt}")

    _emit_ln(g, f"ln2_{fc}", fpar, False, [f"x2ev{fc}_{15}"], ln2_out)

    for dt in range(NDT):
        bank = 4 + dt % 2
        P.emit("tensor",
               lambda e, dt=dt, bank=bank: e.matmul(
                   pb[bank], c_h128[:], xf[:, dt, :], start=True, stop=True),
               waits=[f"ln2_{fc}out{dt}",
                      W(bank_guard[bank]) if dt < 2 else
                      f"fw1ev{fc}_{dt-2}"],
               tag=f"fw1mm{fc}_{dt}")
        P.emit("scalar",
               lambda e, dt=dt, bank=bank: e.copy(xf[:, dt, :], pb[bank]),
               waits=[f"fw1mm{fc}_{dt}"], tag=f"fw1ev{fc}_{dt}")
    bank_guard[4] = f"fw1ev{fc}_{14}"
    bank_guard[5] = f"fw1ev{fc}_{15}"
    sc = 0
    for ht in (1, 2, 4, 8):
        for gg in range(NDT // (2 * ht)):
            for i in range(ht):
                a = gg * 2 * ht + i
                b2 = a + ht
                w = ([f"fw1ev{fc}_{a}", f"fw1ev{fc}_{b2}"] if ht == 1 else [])
                if sc == 0:
                    w += [W(f"zg{c-2}_63")]
                P.emit("vector",
                       lambda e, a=a, b2=b2: e.tensor_sub(
                           xft, xf[:, a, :], xf[:, b2, :]), waits=w)
                P.emit("vector",
                       lambda e, a=a, b2=b2: e.tensor_add(
                           xf[:, a, :], xf[:, a, :], xf[:, b2, :]))
                P.emit("vector",
                       lambda e, b2=b2: e.tensor_copy(xf[:, b2, :], xft),
                       tag=f"fw2_{fc}_{sc}")
                sc += 1
    P.alias(f"xfdone{fc}", f"fw2_{fc}_{sc-1}")


def _emit_front(g, c):
    P, W, pb = g["P"], g["W"], g["pb"]
    bank_guard, slab_use = g["bank_guard"], g["slab_use"]
    sx, xnb = g["sx"], g["xnb"]
    u_ring, sr_ring = g["u_ring"], g["sr_ring"]
    wslab, wxp_sb, wdt_ring = g["wslab"], g["wxp_sb"], g["wdt_ring"]
    xs_ring, tailb, lnrows = g["xs_ring"], g["tailb"], g["lnrows"]
    dtr, bcsb, yd_r, scrA = g["dtr"], g["bcsb"], g["yd_r"], g["scrA"]
    c_convw, c_convb = g["c_convw"], g["c_convb"]
    c_wnorm, c_misc, c_bdt = g["c_wnorm"], g["c_misc"], g["c_bdt"]
    brep, crep = g["brep"], g["crep"]
    s_win = g["s_win"]
    s_u, s_sr, s_bc, s_wdt = g["s_u"], g["s_sr"], g["s_bc"], g["s_wdt"]
    fc = c - 1
    par = c % 2

    if c >= 1:
        _emit_wout_tail(g, c)

    P.alias(f"ln1_{c}gpw", f"ln2_{fc}sq15" if c >= 1 else "Aneg")
    if c >= 1:
        P.alias(f"ln1_{c}pq0", f"ln2_{fc}qmm14")
        P.alias(f"ln1_{c}pq1", f"ln2_{fc}qmm15")

    def ln1_out(dt):
        P.emit("vector",
               lambda e, dt=dt: e.tensor_add(
                   sx[par][:, dt, :], sx[par][:, dt, :], scrA[:, 1, :]))
        P.emit("vector",
               lambda e, dt=dt: e.scalar_tensor_tensor(
                   xnb[:, dt, :], scrA[:, 1, :], c_wnorm[:, dt:dt + 1],
                   lnrows[:, 2, :], op0=ALU.mult, op1=ALU.mult),
               waits=[W(f"winmm{c-1}_15")] if dt == 0 else [],
               tag=f"ln1_{c}out{dt}")

    _emit_ln(g, f"ln1_{c}", par, True,
             ["xld0" if c == 0 else f"xld{c}",
              "@consts" if c == 0 else None],
             ln1_out,
             row0_waits=(["Aexp"] if c == 0 else []))

    def emit_xp(db):
        P.emit("tensor",
               lambda e, db=db: e.matmul(
                   pb[4], wxp_sb[:, db, 0:128], u_ring[:, db % 4, :],
                   start=(db == 0), stop=(db == NDB - 1)),
               waits=[f"usil{c}_{db}",
                      "ldWxp" if db == 0 and c == 0 else None,
                      W(bank_guard[4]) if db == 0 else None],
               tag=f"xp{c}_{db}")
        P.emit("tensor",
               lambda e, db=db: e.matmul(
                   pb[5][0:32, :], wxp_sb[:, db, 128:160],
                   u_ring[:, db % 4, :],
                   start=(db == 0), stop=(db == NDB - 1)),
               waits=[W(bank_guard[5]) if db == 0 else None],
               tag=f"bck{c}_{db}")

    for pg in range(16):
        slot, ldtag, n = slab_use(
            s_win[pg].rearrange("(t p) c -> p t c", p=128),
            [f"cWin{pg}" if c == 0 else None])
        for po in range(4):
            bank = po
            for k in range(NDT):
                P.emit("tensor",
                       lambda e, po=po, k=k, slot=slot: e.matmul(
                           pb[po], wslab[slot][:, k, po * 128:(po + 1) * 128],
                           xnb[:, k, :], start=(k == 0), stop=(k == NDT - 1)),
                       waits=[ldtag if k == 0 and po == 0 else None,
                              f"ln1_{c}out{NDT-1}" if k == 0 and po == 0
                              else None,
                              W(bank_guard[bank]) if k == 0 else None],
                       tag=f"winmm{c}_{pg}_{po}" if k == NDT - 1 else None)
        if 1 <= pg <= 8:
            for dbp in range((pg - 1) * 4, pg * 4):
                emit_xp(dbp)
        for po in range(4):
            db = (pg % 8) * 4 + po
            bank = po
            if pg < 8:
                P.emit("scalar",
                       lambda e, db=db, po=po: e.copy(
                           xs_ring[:, db % 2, 3:3 + LC], pb[po]),
                       waits=[f"winmm{c}_{pg}_{po}",
                              f"convd{c}_{db-2}" if db >= 2 else
                              W(f"convd{c-1}_{db+30}")],
                       tag=f"xsev{c}_{db}")
                bank_guard[bank] = f"xsev{c}_{db}"
                P.emit("vector",
                       lambda e, db=db: e.tensor_copy(xs_ring[:, db % 2, 0:3],
                                                      tailb[:, db, :]),
                       waits=[f"xsev{c}_{db}"])
                P.emit("vector",
                       lambda e, db=db: e.tensor_copy(
                           tailb[:, db, :], xs_ring[:, db % 2, LC:LC + 3]))
                P.emit("vector",
                       lambda e, db=db: e.tensor_scalar(
                           yd_r[:, 0, :], xs_ring[:, db % 2, 0:LC],
                           c_convw[:, db, 0:1], None, ALU.mult))
                P.emit("vector",
                       lambda e, db=db: e.scalar_tensor_tensor(
                           yd_r[:, 1, :], xs_ring[:, db % 2, 1:1 + LC],
                           c_convw[:, db, 1:2], yd_r[:, 0, :],
                           op0=ALU.mult, op1=ALU.add))
                P.emit("vector",
                       lambda e, db=db: e.scalar_tensor_tensor(
                           yd_r[:, 0, :], xs_ring[:, db % 2, 2:2 + LC],
                           c_convw[:, db, 2:3], yd_r[:, 1, :],
                           op0=ALU.mult, op1=ALU.add))
                P.emit("vector",
                       lambda e, db=db: e.scalar_tensor_tensor(
                           yd_r[:, 1, :], xs_ring[:, db % 2, 3:3 + LC],
                           c_convw[:, db, 3:4], yd_r[:, 0, :],
                           op0=ALU.mult, op1=ALU.add),
                       tag=f"convd{c}_{db}")
                uw = [f"convd{c}_{db}"]
                if db >= 4:
                    uw += [f"ust{c}_{db-4}", f"bck{c}_{db-4}"]
                else:
                    uw += [W(f"ybst{c-1}_{28+db}")]
                P.emit("scalar",
                       lambda e, db=db: e.activation(
                           u_ring[:, db % 4, :], yd_r[:, 1, :], AF.Silu,
                           bias=c_convb[:, db:db + 1]),
                       waits=uw, tag=f"usil{c}_{db}")
                P.emit("scalar",
                       lambda e, db=db: e.dma_start(out=s_u[db],
                                                    in_=u_ring[:, db % 4, :]),
                       waits=[f"usil{c}_{db}"], tag=f"ust{c}_{db}", dma=True)
            else:
                db_r = (pg - 8) * 4 + po
                sw = [f"winmm{c}_{pg}_{po}"]
                if db_r >= 3:
                    sw += [f"sst{c}_{db_r-3}"]
                else:
                    last = {0: 30, 1: 31, 2: 29}[db_r]
                    sw += [W(f"ybar{c-1}_{last}")]
                P.emit("scalar",
                       lambda e, db_r=db_r, po=po: e.activation(
                           sr_ring[:, db_r % 3, :], pb[po], AF.Silu),
                       waits=sw, tag=f"srsil{c}_{db_r}")
                bank_guard[bank] = f"srsil{c}_{db_r}"
                P.emit("scalar",
                       lambda e, db_r=db_r: e.dma_start(
                           out=s_sr[db_r], in_=sr_ring[:, db_r % 3, :]),
                       waits=[f"srsil{c}_{db_r}"], tag=f"sst{c}_{db_r}",
                       dma=True)
        P.alias(f"slm{n}", f"winmm{c}_{pg}_3")
        if pg == 15:
            P.alias(f"winmm{c}_15", f"winmm{c}_{pg}_3")

    P.emit("scalar", lambda e: e.copy(dtr[:], pb[4]),
           waits=[f"xp{c}_{NDB-1}", W(f"dtmm{c-1}_31")], tag=f"dtrev{c}")
    bank_guard[4] = f"dtrev{c}"
    P.emit("vector", lambda e: e.tensor_copy(bcsb[:], pb[5][0:32, :]),
           waits=[f"bck{c}_{NDB-1}", W(f"bcst{c-1}")], tag=f"bcev{c}")
    bank_guard[5] = f"bcev{c}"
    P.emit("scalar", lambda e: e.dma_start(out=s_bc, in_=bcsb[:]),
           waits=[f"bcev{c}"], tag=f"bcst{c}", dma=True)
    for n in range(DS):
        src_b = AP(s_bc.tensor, n * 512, [[0, 128], [1, 512]])
        src_c = AP(s_bc.tensor, (16 + n) * 512, [[0, 128], [1, 512]])
        P.emit("gpsimd",
               lambda e, n=n, src_b=src_b: e.dma_start(out=brep[:, n, :],
                                                       in_=src_b),
               waits=[f"bcst{c}", W(f"dBu{c-1}_{496+n}")],
               tag=f"br{c}_{n}", dma=True)
        P.emit("gpsimd",
               lambda e, n=n, src_c=src_c: e.dma_start(out=crep[:, n, :],
                                                       in_=src_c),
               waits=[W(f"pm{c-1}_{496+n}")],
               tag=f"cr{c}_{n}", dma=True)

    for db in range(3):
        lastw = {0: 30, 1: 31, 2: 29}[db]
        P.emit("scalar",
               lambda e, db=db: e.dma_start(
                   out=wdt_ring[:, db % 3, :],
                   in_=s_wdt[:, db * 128:(db + 1) * 128]),
               waits=["cWdt" if c == 0 else None, W(f"dtmm{c-1}_{lastw}")],
               tag=f"wdtld{c}_{db}", dma=True)
    for db in range(2):
        P.emit("tensor",
               lambda e, db=db: e.matmul(
                   pb[db], wdt_ring[:, db % 3, :], dtr[:],
                   start=True, stop=True),
               waits=[f"wdtld{c}_{db}", f"dtrev{c}",
                      W(f"relu{c-1}_{30+db}"), W(bank_guard[db])],
               tag=f"dtmm{c}_{db}")
    P.emit("scalar",
           lambda e: e.activation(scrA[:, 0, :], pb[0], AF.Abs,
                                  bias=c_bdt[:, 0:1]),
           waits=[f"dtmm{c}_0", f"ln1_{c}out{NDT-1}",
                  W(f"delta{c-1}_30")], tag=f"abs{c}_0")
    P.emit("scalar",
           lambda e: e.activation(scrA[:, 2, :], scrA[:, 0, :], AF.Exp,
                                  scale=c_misc[:, 3:4]),
           waits=[W(f"delta{c-1}_30")])
    P.emit("scalar",
           lambda e: e.activation(scrA[:, 0, :], scrA[:, 2, :], AF.Ln,
                                  bias=c_misc[:, 1:2]))
    P.emit("scalar",
           lambda e: e.activation(scrA[:, 2, :], pb[0], AF.Relu,
                                  bias=c_bdt[:, 0:1]),
           tag=f"relu{c}_0")
    P.emit("scalar",
           lambda e: e.activation(scrA[:, 1, :], pb[1], AF.Abs,
                                  bias=c_bdt[:, 1:2]),
           waits=[f"dtmm{c}_1", W(f"delta{c-1}_31")], tag=f"abs{c}_1")
    for db in range(4):
        P.emit("scalar",
               lambda e, db=db: e.dma_start(out=u_ring[:, db % 4, :],
                                            in_=s_u[db]),
               waits=[f"ust{c}_{28+db}", f"xp{c}_{28+db}"],
               tag=f"uld{c}_{db}", dma=True)
    for db in range(3):
        last = {0: 30, 1: 31, 2: 29}[db]
        P.emit("scalar",
               lambda e, db=db: e.dma_start(out=sr_ring[:, db % 3, :],
                                            in_=s_sr[db]),
               waits=[f"sst{c}_{last}"],
               tag=f"srld{c}_{db}", dma=True)


_slabmap = {}


def _build_ffn_groups(g, fc):
    P, W, pb = g["P"], g["W"], g["pb"]
    wslab, xf, zq = g["wslab"], g["xf"], g["zq"]
    zu_r, zg_r, sx = g["zu_r"], g["zg_r"], g["sx"]
    c_susc, c_ggsc, c_dpsc = g["c_susc"], g["c_ggsc"], g["c_dpsc"]
    s_wup, s_wgate, s_G = g["s_wup"], g["s_wgate"], g["s_G"]
    slab_use, bank_guard = g["slab_use"], g["bank_guard"]
    fpar = fc % 2
    c = fc + 1
    groups = []

    def mk_up_gate(pg, po, is_gate):
        j = pg * 4 + po
        pref = "gt" if is_gate else "up"

        def emit(bank):
            if po == 0:
                src = (s_wgate if is_gate else s_wup)[pg].rearrange(
                    "(t p) c -> p t c", p=128)
                extra = [(f"cWgate{pg}" if is_gate else f"cWup{pg}")
                         if fc == 0 else None]
                _slabmap[(pref, fc, pg)] = slab_use(src, extra)
            slot, ldtag, n = _slabmap[(pref, fc, pg)]
            for k in range(NDT):
                P.emit("tensor",
                       lambda e, k=k, slot=slot, po=po, bank=bank: e.matmul(
                           pb[bank], wslab[slot][:, k, po * 128:(po + 1) * 128],
                           xf[:, k, :], start=(k == 0), stop=(k == NDT - 1)),
                       waits=[ldtag if k == 0 else None,
                              f"xfdone{fc}" if k == 0 else None,
                              W(bank_guard[bank]) if k == 0 else None],
                       tag=f"{pref}mm{fc}_{j}" if k == NDT - 1 else None)
            if po == 3:
                P.alias(f"slm{n}", f"{pref}mm{fc}_{j}")

        def evac(bank):
            if is_gate:
                P.emit("scalar",
                       lambda e, bank=bank: e.activation(
                           zg_r[:, j % 2, :], pb[bank], AF.Silu,
                           scale=c_ggsc[:, j:j + 1]),
                       waits=[f"gtmm{fc}_{j}",
                              W(f"z{fc}_{j-2}") if j >= 2 else None],
                       tag=f"zg{fc}_{j}")
                return f"zg{fc}_{j}"
            P.emit("scalar",
                   lambda e, bank=bank: e.mul(zu_r[:, j % 2, :], pb[bank],
                                              c_susc[:, j:j + 1]),
                   waits=[f"upmm{fc}_{j}",
                          W(f"z{fc}_{j-2}") if j >= 2 else None,
                          W(f"ln1_{c}qmm{14+j}") if j < 2 else None,
                          W(f"ln2_{fc}qmm{14+j}") if j < 2 else None],
                   tag=f"zu{fc}_{j}")
            return f"zu{fc}_{j}"

        def zmul():
            q = j // 16
            zw = [f"zu{fc}_{j}", f"zg{fc}_{j}"]
            if q >= 1:
                zw += [f"dnmm{fc}_{q-1}_15"]
            elif fc == 0:
                zw += ["@sync", W("f1mm63")]
            else:
                zw += [W(f"dnmm{fc-1}_3_15")]
            P.emit("vector",
                   lambda e: e.tensor_mul(zq[:, j % 16, :], zu_r[:, j % 2, :],
                                          zg_r[:, j % 2, :]),
                   waits=zw, tag=f"z{fc}_{j}")

        return dict(kind="gate" if is_gate else "ud", emit=emit, evac=evac,
                    post=zmul if is_gate else None)

    def mk_down(q, dt):
        def emit(bank):
            if dt % 4 == 0:
                pog = dt // 4
                _slabmap[("dn", fc, q, pog)] = slab_use(
                    s_G[pog].rearrange("(t p) c -> p t c", p=128)
                    [:, q * 16:(q + 1) * 16, :],
                    ["@sync" if fc == 0 and q == 0 and pog == 0 else None])
            slot, ldtag, n = _slabmap[("dn", fc, q, dt // 4)]
            po = dt % 4
            for k in range(16):
                P.emit("tensor",
                       lambda e, k=k, slot=slot, po=po, bank=bank: e.matmul(
                           pb[bank], wslab[slot][:, k, po * 128:(po + 1) * 128],
                           zq[:, k, :], start=(k == 0), stop=(k == 15)),
                       waits=[ldtag if k == 0 else None,
                              f"z{fc}_{q*16+15}" if k == 0 else None,
                              W(bank_guard[bank]) if k == 0 else None],
                       tag=f"dnmm{fc}_{q}_{dt}" if k == 15 else None)
            if dt % 4 == 3:
                P.alias(f"slm{n}", f"dnmm{fc}_{q}_{dt}")

        def evac(bank):
            dw = [f"dnmm{fc}_{q}_{dt}"]
            if q == 0:
                dw += [f"x2ev{fc}_{dt}", f"ln2_{fc}srd{dt}",
                       f"ln2_{fc}qmm15", f"ln2_{fc}sq15"]
            P.emit("vector",
                   lambda e, bank=bank: e.scalar_tensor_tensor(
                       sx[fpar][:, dt, :], pb[bank], c_dpsc[:, dt:dt + 1],
                       sx[fpar][:, dt, :], op0=ALU.mult, op1=ALU.add),
                   waits=dw, tag=f"dnst{fc}_{q}_{dt}")
            return f"dnst{fc}_{q}_{dt}"

        return dict(kind="ud", emit=emit, evac=evac, post=None)

    for q in range(4):
        for pg in range(q * 4, q * 4 + 4):
            for po in range(4):
                groups.append(mk_up_gate(pg, po, False))
                groups.append(mk_up_gate(pg, po, True))
        for dt in range(16):
            groups.append(mk_down(q, dt))
    P.alias(f"xffree{fc}", f"gtmm{fc}_63") if False else None
    return groups


def _build_fold_groups(g):
    P, W, pb = g["P"], g["W"], g["pb"]
    fold_in, fold_t = g["fold_in"], g["fold_t"]
    fold_s2, fold_t2 = g["fold_s2"], g["fold_t2"]
    c_h128, c_h64 = g["c_h128"], g["c_h64"]
    d_wdownT, s_T, s_G = g["d_wdownT"], g["s_T"], g["s_G"]
    bank_guard = g["bank_guard"]
    groups = []

    def mk_st1(j1):
        r = j1 % 2

        def emit4():
            P.emit("gpsimd",
                   lambda e: e.dma_start(
                       out=fold_in[r],
                       in_=d_wdownT[j1 * 128:(j1 + 1) * 128, :]),
                   waits=(["@consts"] if j1 == 0 else []) + [W(f"f1mm{j1-2}")],
                   tag=f"f1ld{j1}", dma=True)
            for cc in range(4):
                bank = 2 + cc
                mtag = f"f1mm{j1}" if cc == 3 else f"f1mm{j1}_{cc}"
                P.emit("tensor",
                       lambda e, cc=cc, bank=bank: e.matmul(
                           pb[bank], c_h128[:],
                           fold_in[r][:, cc * 512:(cc + 1) * 512],
                           start=True, stop=True),
                       waits=[f"f1ld{j1}" if cc == 0 else None,
                              W(bank_guard[bank])],
                       tag=mtag)
                P.emit("scalar",
                       lambda e, cc=cc, bank=bank: e.copy(
                           fold_t[r][:, cc * 512:(cc + 1) * 512], pb[bank]),
                       waits=[mtag, W(f"f1st{j1-2}") if cc == 0 else None],
                       tag=f"f1cp{j1}_{cc}")
                bank_guard[bank] = f"f1cp{j1}_{cc}"
            P.emit("sync",
                   lambda e: e.dma_start(out=s_T[j1], in_=fold_t[r]),
                   waits=[f"f1cp{j1}_3"], tag=f"f1st{j1}", dma=True)
        return emit4

    def mk_st2(k2):
        r = k2 % 2
        s_G_v = s_G.rearrange("g (k1 k2) c -> g k2 k1 c", k2=128)

        def emit4():
            P.emit("sync",
                   lambda e: e.dma_start(out=fold_s2[r], in_=s_T[:, k2, :]),
                   waits=(["@sync"] if k2 == 0 else []) + [W(f"f2mm{k2-2}")],
                   tag=f"f2ld{k2}", dma=True)
            for cc in range(4):
                bank = 2 + cc
                mtag = f"f2mm{k2}" if cc == 3 else f"f2mm{k2}_{cc}"
                P.emit("tensor",
                       lambda e, cc=cc, bank=bank: e.matmul(
                           pb[bank][0:64, :], c_h64[:],
                           fold_s2[r][:, cc * 512:(cc + 1) * 512],
                           start=True, stop=True),
                       waits=[f"f2ld{k2}" if cc == 0 else None,
                              W(bank_guard[bank])],
                       tag=mtag)
                P.emit("scalar",
                       lambda e, cc=cc, bank=bank: e.copy(
                           fold_t2[r][:, cc * 512:(cc + 1) * 512],
                           pb[bank][0:64, :]),
                       waits=[mtag, W(f"f2st{k2-2}_{cc}")],
                       tag=f"f2cp{k2}_{cc}")
                bank_guard[bank] = f"f2cp{k2}_{cc}"
                P.emit("sync",
                       lambda e, cc=cc: e.dma_start(
                           out=s_G_v[cc, k2, :, :],
                           in_=fold_t2[r][:, cc * 512:(cc + 1) * 512]),
                       waits=[f"f2cp{k2}_{cc}"], tag=f"f2st{k2}_{cc}",
                       dma=True)
        return emit4

    for j1 in range(64):
        groups.append(mk_st1(j1))
    for k2 in range(128):
        groups.append(mk_st2(k2))
    return groups


class _GroupRunner:

    def __init__(self, g, groups, fold_mode):
        self.g = g
        self.groups = groups
        self.fold_mode = fold_mode
        self.gi = 0
        self.bank_cur = {"ud": 0, "gate": 0}
        self.pend = []

    def emit_group(self):
        if self.gi >= len(self.groups):
            return
        grp = self.groups[self.gi]
        self.gi += 1
        if self.fold_mode:
            grp()
            return
        self.pop_evacs()
        if grp["kind"] == "gate":
            bank = (4, 5)[self.bank_cur["gate"] % 2]
            self.bank_cur["gate"] += 1
        else:
            bank = (2, 3)[self.bank_cur["ud"] % 2]
            self.bank_cur["ud"] += 1
        grp["emit"](bank)
        self.pend.append((grp, bank))

    def pop_evacs(self):
        while self.pend:
            grp, bank = self.pend.pop(0)
            tag = grp["evac"](bank)
            self.g["bank_guard"][bank] = tag
            if grp["post"] is not None:
                grp["post"]()

    def done(self):
        return self.gi >= len(self.groups) and not self.pend


def _emit_window(g, c):
    P, W, pb = g["P"], g["W"], g["pb"]
    bank_guard = g["bank_guard"]
    sx, u_ring, sr_ring = g["sx"], g["u_ring"], g["sr_ring"]
    brep, crep = g["brep"], g["crep"]
    wdt_ring, dtr = g["wdt_ring"], g["dtr"]
    delta_r, du_r = g["delta_r"], g["du_r"]
    dA_r, dBu_r, htr, p_r = g["dA_r"], g["dBu_r"], g["htr"], g["p_r"]
    hcarry, yd_r, scrA = g["hcarry"], g["yd_r"], g["scrA"]
    c_A, c_bdt, c_D, c_iden, c_misc = (g["c_A"], g["c_bdt"], g["c_D"],
                                       g["c_iden"], g["c_misc"])
    s_u, s_sr, s_yb, s_wdt = g["s_u"], g["s_sr"], g["s_yb"], g["s_wdt"]
    d_x, d_y = g["d_x"], g["d_y"]
    hc = hcarry.ap()
    fc = c - 1
    fpar = fc % 2

    if c == 0:
        runner = _GroupRunner(g, _build_fold_groups(g), True)
    else:
        runner = _GroupRunner(g, _build_ffn_groups(g, fc), False)
    store_done = [False]

    def try_store():
        if store_done[0] or c == 0 or not P.has(f"dnst{fc}_3_15"):
            return
        store_done[0] = True
        yv = d_y.rearrange("(t p) l -> p t l", p=128)[
            :, :, fc * LC:(fc + 1) * LC]
        P.emit("gpsimd",
               lambda e, yv=yv: e.dma_start(
                   out=yv, in_=sx[fpar].ap().rearrange("p a b -> p (a b)")),
               waits=[f"dnst{fc}_3_{dt}" for dt in range(NDT)],
               tag=f"stored{fc}", dma=True)
        if c + 1 < NCH:
            xv = d_x.rearrange("(t p) l -> p t l", p=128)[
                :, :, (c + 1) * LC:(c + 2) * LC]
            P.emit("gpsimd",
                   lambda e, xv=xv: e.dma_start(
                       out=sx[fpar].ap().rearrange("p a b -> p (a b)"),
                       in_=xv),
                   waits=[f"stored{fc}"], tag=f"xld{c+1}", dma=True)

    for db in range(NDB):
        par = db % 2
        P.emit("vector",
               lambda e, db=db, par=par: e.tensor_add(
                   delta_r[:, par, :], scrA[:, par, :], scrA[:, 2 + par, :]),
               waits=[f"relu{c}_{db}",
                      f"dA{c}_{(db-2)*16+15}" if db >= 2 else
                      W(f"dA{c-1}_{(30+db)*16+15}")],
               tag=f"delta{c}_{db}")
        P.emit("vector",
               lambda e, db=db, par=par: e.tensor_mul(
                   du_r[:, par, :], delta_r[:, par, :], u_ring[:, db % 4, :]),
               waits=[f"uld{c}_{db}",
                      f"dBu{c}_{(db-2)*16+15}" if db >= 2 else
                      W(f"dBu{c-1}_{(30+db)*16+15}")],
               tag=f"du{c}_{db}")
        if db + 2 < NDB:
            db2 = db + 2
            P.emit("tensor",
                   lambda e, db2=db2: e.matmul(
                       pb[db2 % 2], wdt_ring[:, db2 % 3, :], dtr[:],
                       start=True, stop=True),
                   waits=[f"wdtld{c}_{db2}", f"relu{c}_{db}"],
                   tag=f"dtmm{c}_{db2}")
        runner.pop_evacs()
        if db + 1 < NDB:
            db1 = db + 1
            p1 = db1 % 2
            P.emit("scalar",
                   lambda e, db1=db1, p1=p1: e.activation(
                       scrA[:, 2 + p1, :], scrA[:, p1, :], AF.Exp,
                       scale=c_misc[:, 3:4]),
                   waits=[f"abs{c}_{db1}",
                          f"delta{c}_{db1-2}" if db1 >= 2 else
                          W(f"delta{c-1}_{30+db1}")])
            P.emit("scalar",
                   lambda e, db1=db1, p1=p1: e.activation(
                       scrA[:, p1, :], scrA[:, 2 + p1, :], AF.Ln,
                       bias=c_misc[:, 1:2]))
            P.emit("scalar",
                   lambda e, db1=db1, p1=p1: e.activation(
                       scrA[:, 2 + p1, :], pb[p1], AF.Relu,
                       bias=c_bdt[:, db1:db1 + 1]),
                   tag=f"relu{c}_{db1}")
        for n in range(DS):
            lane = db * DS + n
            r = lane % 2
            ybank = 6 + db % 2
            P.emit("scalar",
                   lambda e, db=db, n=n, r=r, par=par: e.activation(
                       dA_r[:, r, :], delta_r[:, par, :], AF.Exp,
                       scale=c_A[:, db, n:n + 1]),
                   waits=["Aneg" if c == 0 and lane == 0 else None,
                          f"delta{c}_{db}" if n == 0 else None,
                          f"tts{c}_{lane-2}" if lane >= 2 else
                          W(f"tts{c-1}_{510+lane}")],
                   tag=f"dA{c}_{lane}")
            P.emit("gpsimd",
                   lambda e, db=db, n=n, r=r, par=par: e.tensor_mul(
                       dBu_r[:, r, :], du_r[:, par, :], brep[:, n, :]),
                   waits=[f"du{c}_{db}" if n == 0 else None,
                          f"br{c}_{n}" if db == 0 else None,
                          f"tts{c}_{lane-2}" if lane >= 2 else
                          W(f"tts{c-1}_{510+lane}")],
                   tag=f"dBu{c}_{lane}")
            P.emit("vector",
                   lambda e, r=r, lane=lane: e.tensor_tensor_scan(
                       htr[:, r, :], dA_r[:, r, :], dBu_r[:, r, :],
                       hc[:, lane:lane + 1], op0=ALU.mult, op1=ALU.add),
                   waits=[f"dA{c}_{lane}", f"dBu{c}_{lane}"],
                   tag=f"tts{c}_{lane}")
            P.emit("vector",
                   lambda e, r=r, n=n, lane=lane: e.tensor_mul(
                       p_r[:, lane % 3, :], htr[:, r, :], crep[:, n, :]),
                   waits=[f"cr{c}_{n}" if db == 0 else None,
                          f"nacc{c}_{lane-3}" if lane >= 3 else
                          W(f"nacc{c-1}_{509+lane}")],
                   tag=f"pm{c}_{lane}")
            P.emit("vector",
                   lambda e, r=r, lane=lane: e.tensor_copy(
                       hc[:, lane:lane + 1], htr[:, r, LC - 1:LC]))
            P.emit("tensor",
                   lambda e, n=n, lane=lane, ybank=ybank: e.matmul(
                       pb[ybank], c_iden[:], p_r[:, lane % 3, :],
                       start=(n == 0), stop=(n == DS - 1)),
                   waits=[f"pm{c}_{lane}",
                          (f"ydst{c}_{db-2}" if db >= 2 else
                           W(f"ydst{c-1}_{30+db}")) if n == 0 else None],
                   tag=f"nacc{c}_{lane}")
            if n in (1, 4, 7, 10, 13):
                runner.emit_group()
        P.emit("vector",
               lambda e, db=db, par=par: e.scalar_tensor_tensor(
                   yd_r[:, par, :], u_ring[:, db % 4, :], c_D[:, db:db + 1],
                   pb[6 + par], op0=ALU.mult, op1=ALU.add),
               waits=[f"nacc{c}_{db*DS+DS-1}"],
               tag=f"ydst{c}_{db}")
        P.emit("vector",
               lambda e, db=db, par=par: e.tensor_mul(
                   u_ring[:, db % 4, :], yd_r[:, par, :],
                   sr_ring[:, db % 3, :]),
               waits=[f"srld{c}_{db}"], tag=f"ybar{c}_{db}")
        P.emit("scalar",
               lambda e, db=db: e.dma_start(out=s_yb[db],
                                            in_=u_ring[:, db % 4, :]),
               waits=[f"ybar{c}_{db}"], tag=f"ybst{c}_{db}", dma=True)
        if db + 2 < NDB:
            P.emit("scalar",
                   lambda e, db=db: e.activation(
                       scrA[:, db % 2, :], pb[db % 2], AF.Abs,
                       bias=c_bdt[:, db + 2:db + 3]),
                   waits=[f"dtmm{c}_{db+2}", f"delta{c}_{db}"],
                   tag=f"abs{c}_{db+2}")
        if db + 3 < NDB:
            db3 = db + 3
            P.emit("scalar",
                   lambda e, db3=db3: e.dma_start(out=sr_ring[:, db3 % 3, :],
                                                  in_=s_sr[db3]),
                   waits=[f"ybar{c}_{db3-3}"], tag=f"srld{c}_{db3}", dma=True)
            P.emit("scalar",
                   lambda e, db3=db3: e.dma_start(
                       out=wdt_ring[:, db3 % 3, :],
                       in_=s_wdt[:, db3 * 128:(db3 + 1) * 128]),
                   waits=[f"dtmm{c}_{db3-3}"],
                   tag=f"wdtld{c}_{db3}", dma=True)
        if db + 4 < NDB:
            db4 = db + 4
            P.emit("scalar",
                   lambda e, db4=db4: e.dma_start(out=u_ring[:, db4 % 4, :],
                                                  in_=s_u[db4]),
                   waits=[f"ybst{c}_{db4-4}"], tag=f"uld{c}_{db4}", dma=True)
        runner.pop_evacs()
        runner.emit_group()
        try_store()
    while not runner.done():
        runner.emit_group()
        runner.pop_evacs()
    try_store()
    bank_guard[0] = f"relu{c}_30"
    bank_guard[1] = f"relu{c}_31"
    bank_guard[6] = f"ydst{c}_30"
    bank_guard[7] = f"ydst{c}_31"
    if c == 0:
        g["P"].alias("Gdone", "f2st127_3")
        xv = d_x.rearrange("(t p) l -> p t l", p=128)[:, :, LC:2 * LC]
        P.emit("gpsimd",
               lambda e, xv=xv: e.dma_start(
                   out=sx[1].ap().rearrange("p a b -> p (a b)"), in_=xv),
               tag="xld1", dma=True)
    else:
        g["P"].alias(f"xffree{fc}", f"gtmm{fc}_63")
        g["P"].alias(f"dnlast{fc}", f"dnmm{fc}_3_15")


def _emit_drain(g):
    P = g["P"]
    sx, d_y = g["sx"], g["d_y"]
    fc = NCH - 1
    fpar = fc % 2
    _emit_wout_tail(g, NCH)
    runner = _GroupRunner(g, _build_ffn_groups(g, fc), False)
    cnt = 0
    while not runner.done():
        runner.emit_group()
        cnt += 1
        if cnt % 2 == 0 or runner.gi >= len(runner.groups):
            runner.pop_evacs()
    P.alias(f"xffree{fc}", f"gtmm{fc}_63")
    P.alias(f"dnlast{fc}", f"dnmm{fc}_3_15")
    yv = d_y.rearrange("(t p) l -> p t l", p=128)[:, :, fc * LC:]
    P.emit("gpsimd",
           lambda e, yv=yv: e.dma_start(
               out=yv, in_=sx[fpar].ap().rearrange("p a b -> p (a b)")),
           waits=[f"dnst{fc}_3_{dt}" for dt in range(NDT)],
           tag=f"stored{fc}", dma=True)


_NC_CACHE = None


def _get_nc():
    global _NC_CACHE
    if _NC_CACHE is None:
        _NC_CACHE = build_nc()
    return _NC_CACHE


def kernel(**inputs):
    x = np.asarray(inputs["L0_attn_x"], np.float32)
    W_in = np.asarray(inputs["L0_attn_W_in"], np.float32)
    W_xp = np.asarray(inputs["L0_attn_W_xp"], np.float32)
    W_dt = np.asarray(inputs["L0_attn_W_dt"], np.float32)
    W_out = np.asarray(inputs["L0_attn_W_out"], np.float32)
    W_up = np.asarray(inputs["L0_ffn_W_up"], np.float32)
    W_gate = np.asarray(inputs["L0_ffn_W_gate"], np.float32)
    W_down = np.asarray(inputs["L0_ffn_W_down"], np.float32)
    A_log = np.asarray(inputs["L0_attn_A_log"], np.float32)
    conv_w = np.asarray(inputs["L0_attn_conv_w"], np.float32)
    conv_b = np.asarray(inputs["L0_attn_conv_b"], np.float32)
    b_dt = np.asarray(inputs["L0_attn_b_dt"], np.float32)
    Dv = np.asarray(inputs["L0_attn_D"], np.float32)
    w_norm = np.asarray(inputs["L0_attn_w_norm"], np.float32)
    SU_ug = np.asarray(inputs["L0_ffn_SU_ug"], np.float32)
    SU_dp = np.asarray(inputs["L0_ffn_SU_dp"], np.float32)
    Wsc_ug = np.asarray(inputs["L0_ffn_Wscale_ug"], np.float32)
    Wsc_dp = np.asarray(inputs["L0_ffn_Wscale_dp"], np.float32)

    H128 = _hadamard(128)
    H64 = _hadamard(64)

    common = {
        "W_in": W_in,
        "W_xp": W_xp,
        "W_dt": W_dt,
        "W_out": W_out,
        "W_upT": np.ascontiguousarray(W_up.T),
        "W_gateT": np.ascontiguousarray(W_gate.T),
        "W_downT": np.ascontiguousarray(W_down.T),
        "A_log_r": np.ascontiguousarray(A_log.reshape(NDB, 128, DS)
                                        .transpose(1, 0, 2)),
        "conv_w_r": np.ascontiguousarray(conv_w.T.reshape(NDB, 128, 4)
                                         .transpose(1, 0, 2)),
        "conv_b_r": np.ascontiguousarray(conv_b.reshape(NDB, 128).T),
        "b_dt_r": np.ascontiguousarray(b_dt.reshape(NDB, 128).T),
        "D_r": np.ascontiguousarray(Dv.reshape(NDB, 128).T),
        "w_norm_r": np.ascontiguousarray(w_norm.reshape(NDT, 128).T),
        "SU_ug_r": np.ascontiguousarray(SU_ug.reshape(NDT, 128).T),
        "SU_dp_r": np.ascontiguousarray(SU_dp.reshape(NJ, 128).T),
        "Wsc_up_r": np.ascontiguousarray(Wsc_ug[:8192].reshape(NJ, 128).T),
        "Wsc_gt_r": np.ascontiguousarray(Wsc_ug[8192:].reshape(NJ, 128).T),
        "Wsc_dp_r": np.ascontiguousarray(Wsc_dp.reshape(NDT, 128).T),
        "H128c": (H128 * 2.0 ** -4).astype(BF),
        "H64": H64.astype(BF),
        "iden128": np.eye(128, dtype=BF),
        "ones_bf": np.ones((128, 1), BF),
        "ones1x128_f": np.ones((1, 128), np.float32),
        "misc_c": np.tile(np.array([[EPS, 1.0, -0.5, -1.0]], np.float32),
                          (128, 1)),
    }
    in_maps = []
    for b in range(8):
        m = dict(common)
        m["xT"] = np.ascontiguousarray(x[b].T)
        in_maps.append(m)

    nc = _get_nc()
    res = run_bass_kernel_spmd(nc, in_maps, core_ids=list(range(8)))
    out = np.empty((8, L, DM), np.float32)
    for b in range(8):
        out[b] = np.asarray(res.results[b]["y_out"]).T
    return out
